# version 06 (final)
# baseline (speedup 1.0000x reference)
import numpy as np
import ml_dtypes
import concourse.bass as bass
import concourse.mybir as mybir
from concourse.bass_utils import run_bass_kernel_spmd

F32 = mybir.dt.float32
BF16 = mybir.dt.bfloat16
I32 = mybir.dt.int32
AF = mybir.ActivationFunctionType
ALU = mybir.AluOpType

B = 2
N_SEQ = 8192
D = 2048
H = 16
HD = 128
T = 2048
NT = 16
NF = 16
NCH = 32
E = 16
NI = 11
NIS = 22
EPS = 1e-5

ALL_PB = [f"PB{i}" for i in range(8)]


class Plan:
    ENGINES = ("sync", "gpsimd", "scalar", "vector", "tensor")

    def __init__(self, nc):
        self.nc = nc
        self.steps = []

    def step(self, eng, fn, reads=(), writes=(), dma=False, tag=""):
        assert eng in self.ENGINES
        self.steps.append(dict(eng=eng, fn=fn, reads=tuple(reads),
                               writes=tuple(writes), dma=dma, tag=tag,
                               idx=len(self.steps)))

    def dma(self, eng, out, in_, reads=(), writes=(), tag="", **kw):
        self.step(eng, lambda e: e.dma_start(out=out, in_=in_, **kw),
                  reads=reads, writes=writes, dma=True, tag=tag)

    def emit(self, block):
        n = len(self.steps)
        last_writer = {}
        readers_since = {}
        deps = [set() for _ in range(n)]
        for st in self.steps:
            i = st["idx"]
            for r in st["reads"]:
                w = last_writer.get(r)
                if w is not None:
                    deps[i].add(w)
                readers_since.setdefault(r, []).append(i)
            for r in st["writes"]:
                w = last_writer.get(r)
                if w is not None:
                    deps[i].add(w)
                for rd in readers_since.get(r, ()):
                    if rd != i:
                        deps[i].add(rd)
                readers_since[r] = []
                last_writer[r] = i
        unsafe_same = ("vector", "scalar", "gpsimd")
        for i, st in enumerate(self.steps):
            deps[i] = {j for j in deps[i]
                       if self.steps[j]["eng"] != st["eng"]
                       or self.steps[j]["dma"]
                       or (st["eng"] in unsafe_same and not st["dma"])}
        marked = [st["dma"] for st in self.steps]
        for i in range(n):
            for j in deps[i]:
                marked[j] = True
        cum = [0] * n
        tot = {e: 0 for e in self.ENGINES}
        for i, st in enumerate(self.steps):
            if marked[i]:
                tot[st["eng"]] += 16 if st["dma"] else 1
            cum[i] = tot[st["eng"]]
        sems = {e: self.nc.alloc_semaphore(f"E_{e}") for e in self.ENGINES}
        plan_steps = self.steps

        def body_for(name):
            def body(eng):
                watermark = {e: 0 for e in self.ENGINES}
                for st in plan_steps:
                    if st["eng"] != name:
                        continue
                    i = st["idx"]
                    need = {}
                    for j in deps[i]:
                        pe = plan_steps[j]["eng"]
                        need[pe] = max(need.get(pe, 0), cum[j])
                    for pe, val in sorted(need.items()):
                        if watermark[pe] < val:
                            eng.wait_ge(sems[pe], val)
                            watermark[pe] = val
                    inst = st["fn"](eng)
                    if marked[i]:
                        assert inst is not None, f"marked step {st['tag']} -> None"
                        inst.then_inc(sems[name], 16 if st["dma"] else 1)
            return body

        for name in self.ENGINES:
            getattr(block, name)(body_for(name))


PHASE_MARKS = {}
BUILD_LIMIT = None
DEBUG_TAP = None


def build_program():
    nc = bass.Bass("TRN2", target_bir_lowering=False, debug=False)

    def din(name, shape, dt=F32):
        return nc.dram_tensor(name, shape, dt, kind="ExternalInput")

    xT_own = din("xT_own", [D, T])
    xT_rep = din("xT_rep", [D, T])
    wq_r = din("wq_r", [H, NF, 128, 128])
    wkv_r = din("wkv_r", [H, NF, 128, 256])
    bq_pp = din("bq_pp", [128, H])
    bkv_row = din("bkv_row", [1, H * 256])
    wout_r = din("wout_r", [NF, 128, D])
    bout_pp = din("bout_pp", [128, NF])
    slopes_pp = din("slopes_pp", [128, H])
    gate_wTd = din("gate_wTd", [D, E])
    sw1_r = din("sw1_r", [NIS, 128, NF, 256])
    sw2_r = din("sw2_r", [NF, 128, NIS, 128])
    w1_r = din("w1_r", [E, NI, 128, NF, 256])
    w2_r = din("w2_r", [E, NF, 128, NI, 128])
    exch_m = din("exch_m", [128, 3])
    exch_d = din("exch_d", [128, 3])
    nonce_in = din("nonce_in", [1, 2], I32)

    yT_out = nc.dram_tensor("yT_out", [D, T], F32, kind="ExternalOutput")

    q_spill = nc.dram_tensor("q_spill", [H, 128, T], BF16, kind="Internal")
    attn_spill = nc.dram_tensor("attn_spill", [H, 128, T], BF16, kind="Internal")
    ao_scr = nc.dram_tensor("ao_scr", [NF, 128, T], F32, kind="Internal")
    x1_scr = nc.dram_tensor("x1_scr", [NF, 128, T], F32, kind="Internal")
    actw_scr = nc.dram_tensor("actw_scr", [E, NI, 128, T], BF16, kind="Internal")
    hsh_scr = nc.dram_tensor("hsh_scr", [NIS, 128, T], BF16, kind="Internal")
    y_scr = nc.dram_tensor("y_scr", [NF, 128, T], F32, kind="Internal")

    srep_loc = nc.dram_tensor("srep_loc", [H, 128, HD], F32, kind="Internal")
    sloc_loc = nc.dram_tensor("sloc_loc", [H, 128, HD], F32, kind="Internal")
    psrep_loc = nc.dram_tensor("psrep_loc", [H, 128, HD], F32, kind="Internal")
    psloc_loc = nc.dram_tensor("psloc_loc", [H, 128, HD], F32, kind="Internal")
    sh_srep = nc.dram_tensor("sh_srep", [2, H, 128, HD], F32, kind="Internal",
                             addr_space="Shared")
    sh_sloc = nc.dram_tensor("sh_sloc", [2, H, 128, HD], F32, kind="Internal",
                             addr_space="Shared")
    sh_flags = nc.dram_tensor("sh_flags", [2, 1, 2], I32, kind="Internal",
                              addr_space="Shared")

    sb = nc.alloc_sbuf_tensor
    A1 = sb("A1", [128, 32768], BF16)
    A2 = sb("A2", [128, 12288], BF16)
    WS = [sb(f"WS{i}", [128, 4096], BF16) for i in range(2)]
    XS = [sb(f"XS{i}", [128, 2048], F32) for i in range(2)]
    OS = [sb(f"OS{i}", [128, 2048], F32) for i in range(2)]
    QH = [sb(f"QH{i}", [128, 2048], BF16) for i in range(2)]
    AH = sb("AH", [128, 2048], BF16)
    SMB = sb("SMB", [128, 2048], BF16)
    SC = sb("SC", [128, 64], BF16)
    QD = sb("QD", [128, 128], BF16)
    S32 = sb("S32", [128, 128], F32)
    SBF = sb("SBF", [128, 128], BF16)
    SST = sb("SST", [128, 128], F32)
    SSB = sb("SSB", [128, 128], BF16)
    SPT = [sb(f"SPT{i}", [128, 128], F32) for i in range(4)]
    IDT = sb("IDT", [128, H * 64], BF16)
    QDB = sb("QDB", [128, H * 64], BF16)
    KDV = sb("KDV", [128, H], F32)
    LAM = sb("LAM", [128, H], F32)
    KDG = sb("KDG", [128, H * NT], F32)
    SLP = sb("SLP", [128, H], F32)
    EXM = sb("EXM", [128, 3], F32)
    EXD = sb("EXD", [128, 3], F32)
    EXF = sb("EXF", [128, 3 * H], F32)
    IDN = sb("IDN", [128, 128], BF16)
    IDNF = sb("IDNF", [128, 128], F32)
    ONE1 = sb("ONE1", [1, 128], BF16)
    ONEC = sb("ONEC", [128, 1], F32)
    ONE1F = sb("ONE1F", [1, 128], F32)
    BQ = sb("BQ", [128, H], F32)
    BO = sb("BO", [128, NF], F32)
    GW = sb("GW", [128, NF * E], BF16)
    PK1 = sb("PK1", [128, 2048], F32)
    PK2 = sb("PK2", [128, 4096], F32)
    PK3 = sb("PK3", [128, 2048], F32)
    ONESB = sb("ONESB", [128, 128], F32)
    MRow = PK1[0:1, :]
    IRow = PK3[0:1, :]
    BKV = PK2[0:1, :]
    SEL = PK2[32:48, 0:2048]
    CMT = PK2[32:48, 2048:4096]
    TMQ = PK2[64:65, 0:2048]
    QGR = PK2[64:65, 2048:4096]
    PRB = sb("PRB", [128, 16], F32)
    MX8 = sb("MX8", [128, 8], F32)
    CMB = sb("CMB", [128, 16], F32)
    NON = sb("NON", [1, 2], I32)
    FLG = sb("FLG", [1, 2], I32)
    TMP = sb("TMP", [128, 1024], F32)
    EPSR = sb("EPSR", [128, 1], F32)
    TMQB = sb("TMQB", [1, 64], BF16)
    PB = [nc.alloc_psum_tensor(f"PB{i}", [128, 512], F32) for i in range(8)]

    p = Plan(nc)

    _regs = {}

    def reg(e, name):
        key = (e.engine, name)
        if key not in _regs:
            _regs[key] = e.alloc_register(name)
        return _regs[key]

    _pidc = {}

    def prow(e):
        if "own" not in _pidc:
            _pidc["own"] = e.snap(e.partition_id(), min_val=0, max_val=7) % 2
        return _pidc["own"]

    def prow_p(e):
        if "par" not in _pidc:
            _pidc["par"] = 1 - prow(e)
        return _pidc["par"]

    def bank(i):
        return PB[i]

    IDT2 = IDT

    def bankbf(i):
        return PB[i].bitcast(BF16)

    def a1t(i):
        return A1[:, i * T:(i + 1) * T]

    def a2t(i):
        return A2[:, i * T:(i + 1) * T]

    A1K = [f"A1.{i}" for i in range(16)]
    A2K = [f"A2.{i}" for i in range(6)]

    p.dma("sync", SLP[:, :], slopes_pp[:, :], writes=["SLP"])
    p.dma("sync", BQ[:, :], bq_pp[:, :], writes=["BQ"])
    p.dma("sync", BKV[:, :], bkv_row[:, :], writes=["BKV"])
    p.dma("sync", BO[:, :], bout_pp[:, :], writes=["BO"])
    p.dma("sync", EXM[:, :], exch_m[:, :], writes=["EXM"])
    p.dma("sync", EXD[:, :], exch_d[:, :], writes=["EXD"])
    p.dma("sync", NON[:, :], nonce_in[:, :], writes=["NON"])
    p.dma("gpsimd", GW[:, :], gate_wTd.rearrange("(a p) e -> p a e", p=128),
          writes=["GW"])

    p.step("gpsimd", lambda e: e.memset(IDN[:, :], 0.0), writes=["IDN"])
    p.step("gpsimd", lambda e: e.affine_select(
        out=IDN[:, :], in_=IDN[:, :], pattern=[[-1, 128]], channel_multiplier=1,
        compare_op=ALU.not_equal, fill=1.0), reads=["IDN"], writes=["IDN"])
    p.step("gpsimd", lambda e: e.memset(IDNF[:, :], 0.0), writes=["IDNF"])
    p.step("gpsimd", lambda e: e.affine_select(
        out=IDNF[:, :], in_=IDNF[:, :], pattern=[[-1, 128]], channel_multiplier=1,
        compare_op=ALU.not_equal, fill=1.0), reads=["IDNF"], writes=["IDNF"])
    p.step("gpsimd", lambda e: e.memset(ONE1[:, :], 1.0), writes=["ONE1"])
    p.step("gpsimd", lambda e: e.memset(ONEC[:, :], 1.0), writes=["ONEC"])
    p.step("gpsimd", lambda e: e.memset(ONE1F[:, :], 1.0), writes=["ONE1F"])
    p.step("gpsimd", lambda e: e.memset(EPSR[:, :], EPS), writes=["EPSR"])
    p.step("gpsimd", lambda e: e.memset(ONESB[:, :], 1.0), writes=["ONESB"])

    p.step("gpsimd", lambda e: e.iota(
        SEL[:, :], pattern=[[1, E * 128]], base=0, channel_multiplier=-128,
        allow_small_or_imprecise_dtypes=True), writes=["SEL"])
    p.step("gpsimd", lambda e: e.affine_select(
        out=SEL[:, :], in_=SEL[:, :], pattern=[[1, E * 128]],
        channel_multiplier=-128, compare_op=ALU.is_ge, fill=-1.0),
        reads=["SEL"], writes=["SEL"])
    p.step("gpsimd", lambda e: e.affine_select(
        out=SEL[:, :], in_=SEL[:, :], pattern=[[-1, E * 128]],
        channel_multiplier=128, base=127, compare_op=ALU.is_ge, fill=-1.0),
        reads=["SEL"], writes=["SEL"])
    p.step("vector", lambda e: e.tensor_single_scalar(
        SEL[:, :], SEL[:, :], -0.5, ALU.is_gt), reads=["SEL"], writes=["SEL"])

    p.step("gpsimd", lambda e: e.iota(TMP[0:64, 0:64], pattern=[[-1, 64]],
                                      base=0, channel_multiplier=1,
                                      allow_small_or_imprecise_dtypes=True),
           writes=["TMPdif"])
    p.step("gpsimd", lambda e: e.iota(TMP[64:128, 0:64], pattern=[[-1, 64]],
                                      base=0, channel_multiplier=1,
                                      allow_small_or_imprecise_dtypes=True),
           writes=["TMPdif2"])
    p.step("gpsimd", lambda e: e.iota(TMP[:, 64:65], pattern=[[0, 1]], base=0,
                                      channel_multiplier=1,
                                      allow_small_or_imprecise_dtypes=True),
           writes=["TMPp"])
    p.step("vector", lambda e: e.tensor_single_scalar(
        TMP[:, 65:66], TMP[:, 64:65], 63.5, ALU.is_gt),
        reads=["TMPp"], writes=["TMPp2"])
    p.step("vector", lambda e: e.tensor_single_scalar(
        TMP[:, 65:66], TMP[:, 65:66], -64.0, ALU.mult),
        reads=["TMPp2"], writes=["TMPp2"])
    p.step("vector", lambda e: e.tensor_tensor(
        TMP[:, 66:67], TMP[:, 64:65], TMP[:, 65:66], ALU.add),
        reads=["TMPp", "TMPp2"], writes=["TMPp3"])
    p.step("vector", lambda e: e.tensor_single_scalar(
        TMP[:, 67:68], TMP[:, 66:67], -64.0, ALU.add),
        reads=["TMPp3"], writes=["TMPp4"])
    p.step("gpsimd", lambda e: e.iota(TMP[:, 80:80 + NT], pattern=[[128, NT]],
                                      base=0, channel_multiplier=1,
                                      allow_small_or_imprecise_dtypes=True),
           writes=["TMPpt"])
    p.step("gpsimd", lambda e: e.iota(TMQ[:, :], pattern=[[1, 2048]], base=0,
                                      channel_multiplier=0,
                                      allow_small_or_imprecise_dtypes=True),
           writes=["TMQpos"])

    for h in range(H):
        p.step("vector", lambda e, h=h: e.tensor_scalar(
            IDT[0:64, h * 64:(h + 1) * 64], TMP[0:64, 0:64],
            SLP[0:64, h:h + 1], None, ALU.mult),
            reads=["TMPdif", "SLP"], writes=[f"IDT{h}"])
        p.step("scalar", lambda e, h=h: e.activation(
            IDT[0:64, h * 64:(h + 1) * 64], IDT[0:64, h * 64:(h + 1) * 64],
            AF.Exp), reads=[f"IDT{h}"], writes=[f"IDT{h}"])
        p.step("gpsimd", lambda e, h=h: e.affine_select(
            out=IDT[0:64, h * 64:(h + 1) * 64],
            in_=IDT[0:64, h * 64:(h + 1) * 64],
            pattern=[[1, 64]], channel_multiplier=-1,
            compare_op=ALU.is_ge, fill=0.0),
            reads=[f"IDT{h}"], writes=[f"IDT{h}"])
        p.step("vector", lambda e, h=h: e.tensor_scalar(
            IDT[64:128, h * 64:(h + 1) * 64], TMP[64:128, 0:64],
            SLP[64:128, h:h + 1], None, ALU.mult),
            reads=["TMPdif2", "SLP"], writes=[f"IDTb{h}"])
        p.step("scalar", lambda e, h=h: e.activation(
            IDT[64:128, h * 64:(h + 1) * 64], IDT[64:128, h * 64:(h + 1) * 64],
            AF.Exp), reads=[f"IDTb{h}"], writes=[f"IDTb{h}"])
        p.step("gpsimd", lambda e, h=h: e.affine_select(
            out=IDT[64:128, h * 64:(h + 1) * 64],
            in_=IDT[64:128, h * 64:(h + 1) * 64],
            pattern=[[1, 64]], channel_multiplier=-1, base=0,
            compare_op=ALU.is_ge, fill=0.0),
            reads=[f"IDTb{h}"], writes=[f"IDT{h}", f"IDTb{h}"])

        p.step("vector", lambda e, h=h: e.tensor_scalar(
            TMP[0:1, 128:192], TMP[0:1, 0:64], SLP[0:1, h:h + 1], None,
            ALU.mult), reads=["TMPdif", "SLP"], writes=["TMPqd"])
        p.step("scalar", lambda e, h=h: e.activation(
            TMP[0:1, 192:256], TMP[0:1, 128:192], AF.Exp),
            reads=["TMPqd"], writes=["TMPqe"])
        p.step("vector", lambda e, h=h: e.tensor_copy(
            TMQB[0:1, 0:64], TMP[0:1, 192:256]),
            reads=["TMPqe"], writes=["TMQB"])
        p.step("tensor", lambda e, h=h: e.matmul(
            bank(0)[:, 0:64], ONE1[:, :], TMQB[0:1, 0:64],
            start=True, stop=True),
            reads=["TMQB", "ONE1"], writes=["PB0"])
        p.step("vector", lambda e, h=h: e.tensor_copy(
            QDB[:, h * 64:(h + 1) * 64], bank(0)[:, 0:64]),
            reads=["PB0"], writes=[f"QDB{h}"])

        p.step("vector", lambda e, h=h: e.tensor_scalar(
            TMP[:, 256 + h:257 + h], TMP[:, 67:68], SLP[:, h:h + 1], None,
            ALU.mult), reads=["TMPp4", "SLP"], writes=[f"TMPk{h}"])
        p.step("scalar", lambda e, h=h: e.activation(
            KDV[:, h:h + 1], TMP[:, 256 + h:257 + h], AF.Exp),
            reads=[f"TMPk{h}"], writes=[f"KDV{h}"])
        p.step("scalar", lambda e, h=h: e.activation(
            LAM[:, h:h + 1], SLP[:, h:h + 1], AF.Exp, bias=0.0, scale=-64.0),
            reads=["SLP"], writes=[f"LAM{h}"])

        p.step("vector", lambda e, h=h: e.tensor_scalar(
            TMP[:, 288 + h * NT:288 + (h + 1) * NT], TMP[:, 80:80 + NT],
            SLP[:, h:h + 1], None, ALU.mult),
            reads=["TMPpt", "SLP"], writes=[f"TMPg{h}"])
        p.step("vector", lambda e, h=h: e.tensor_scalar(
            TMP[:, 800 + h:801 + h], SLP[:, h:h + 1], -float(T), None,
            ALU.mult), reads=["SLP"], writes=[f"TMPt{h}"])
        p.step("scalar", lambda e, h=h: e.activation(
            KDG[:, h * NT:(h + 1) * NT], TMP[:, 288 + h * NT:288 + (h + 1) * NT],
            AF.Exp, bias=TMP[:, 800 + h:801 + h], scale=1.0),
            reads=[f"TMPg{h}", f"TMPt{h}"], writes=[f"KDG{h}"])

        p.step("vector", lambda e, h=h: e.tensor_scalar(
            TMP[:, 832 + 3 * h:835 + 3 * h], EXD[:, :], SLP[:, h:h + 1], None,
            ALU.mult), reads=["EXD", "SLP"], writes=[f"TMPe{h}"])
        p.step("scalar", lambda e, h=h: e.activation(
            EXF[:, 3 * h:3 * h + 3], TMP[:, 832 + 3 * h:835 + 3 * h], AF.Exp,
            bias=0.0, scale=-1.0), reads=[f"TMPe{h}"], writes=[f"EXF{h}"])
        p.step("vector", lambda e, h=h: e.tensor_tensor(
            EXF[:, 3 * h:3 * h + 3], EXF[:, 3 * h:3 * h + 3], EXM[:, :],
            ALU.mult), reads=[f"EXF{h}", "EXM"], writes=[f"EXF{h}"])

    def ln_stats(src_tile_fn, src_keys):
        for ft in range(NF):
            p.dma("sync", XS[ft % 2][:, :], src_tile_fn(ft),
                  reads=src_keys, writes=[f"XS{ft % 2}"])
            p.step("scalar", lambda e, ft=ft: e.activation(
                OS[ft % 2][:, :], XS[ft % 2][:, :], AF.Square),
                reads=[f"XS{ft % 2}"], writes=[f"OS{ft % 2}"])
            def mm_stats(e, ft=ft):
                inst = None
                for c in range(4):
                    e.matmul(bank(c)[0:1, :], ONEC[:, :],
                             XS[ft % 2][:, c * 512:(c + 1) * 512],
                             start=(ft == 0), stop=(ft == NF - 1))
                for c in range(4):
                    inst = e.matmul(bank(4 + c)[0:1, :], ONEC[:, :],
                                    OS[ft % 2][:, c * 512:(c + 1) * 512],
                                    start=(ft == 0), stop=(ft == NF - 1))
                return inst
            wr = ALL_PB + ["PBs"] if ft in (0, NF - 1) else ["PBs"]
            p.step("tensor", mm_stats,
                   reads=[f"XS{ft % 2}", f"OS{ft % 2}", "ONEC"], writes=wr)
        for c in range(4):
            p.step("vector", lambda e, c=c: e.tensor_single_scalar(
                bank(c)[0:1, :], bank(c)[0:1, :], 1.0 / D, ALU.mult),
                reads=[f"PB{c}"], writes=[f"PB{c}"])
            p.step("vector", lambda e, c=c: e.tensor_copy(
                MRow[:, c * 512:(c + 1) * 512], bank(c)[0:1, :]),
                reads=[f"PB{c}"], writes=["MRow"])
            p.step("vector", lambda e, c=c: e.tensor_single_scalar(
                IRow[:, c * 512:(c + 1) * 512], bank(4 + c)[0:1, :], 1.0 / D,
                ALU.mult), reads=[f"PB{4 + c}"], writes=["IRow"])
            p.step("vector", lambda e, c=c: e.tensor_tensor(
                bank(c)[0:1, :], MRow[:, c * 512:(c + 1) * 512],
                MRow[:, c * 512:(c + 1) * 512], ALU.mult),
                reads=["MRow", f"PB{c}"], writes=[f"PB{c}"])
            p.step("vector", lambda e, c=c: e.tensor_tensor(
                IRow[:, c * 512:(c + 1) * 512], IRow[:, c * 512:(c + 1) * 512],
                bank(c)[0:1, :], ALU.subtract),
                reads=["IRow", f"PB{c}"], writes=["IRow"])
        p.step("scalar", lambda e: e.activation(
            IRow[:, :], IRow[:, :], AF.Sqrt, bias=EPSR[0:1, :], scale=1.0),
            reads=["IRow", "EPSR"], writes=["IRow"])
        p.step("vector", lambda e: e.reciprocal(IRow[:, :], IRow[:, :]),
               reads=["IRow"], writes=["IRow"])
        def mm_bc(e):
            inst = None
            for c in range(4):
                e.matmul(bank(c)[:, :], ONE1F[:, :],
                         MRow[:, c * 512:(c + 1) * 512], start=True, stop=True)
            for c in range(4):
                inst = e.matmul(bank(4 + c)[:, :], ONE1F[:, :],
                                IRow[:, c * 512:(c + 1) * 512],
                                start=True, stop=True)
            return inst
        p.step("tensor", mm_bc, reads=["MRow", "IRow", "ONE1F"],
               writes=ALL_PB + ["PBs"])

    def ln_apply(src_tile_fn, src_keys, dst_fn, dst_key_fn):
        for ft in range(NF):
            p.dma("sync", XS[ft % 2][:, :], src_tile_fn(ft),
                  reads=src_keys, writes=[f"XS{ft % 2}"])
            for c in range(4):
                p.step("vector", lambda e, ft=ft, c=c: e.tensor_tensor(
                    OS[ft % 2][:, c * 512:(c + 1) * 512],
                    XS[ft % 2][:, c * 512:(c + 1) * 512], bank(c)[:, :],
                    ALU.subtract),
                    reads=[f"XS{ft % 2}", f"PB{c}"], writes=[f"OS{ft % 2}"])
                p.step("vector", lambda e, ft=ft, c=c: e.tensor_tensor(
                    dst_fn(ft)[:, c * 512:(c + 1) * 512],
                    OS[ft % 2][:, c * 512:(c + 1) * 512], bank(4 + c)[:, :],
                    ALU.mult),
                    reads=[f"OS{ft % 2}", f"PB{4 + c}"], writes=[dst_key_fn(ft)])

    def xo_tile(ft):
        return xT_own[ft * 128:(ft + 1) * 128, :]

    def xr_tile(ft):
        return xT_rep[ft * 128:(ft + 1) * 128, :]

    ln_stats(xr_tile, ["xrep"])
    ln_apply(xr_tile, ["xrep"], a1t, lambda ft: f"A1.{ft}")

    for h in range(H):
        ws = WS[h % 2]
        p.dma("gpsimd", ws[:, 0:NF * 256],
              wkv_r[h].rearrange("a p c -> p a c"),
              reads=["wqkv"], writes=[f"WS{h % 2}"], tag=f"rw{h}")
        for tt in range(NT):
            bk = tt % 2
            def mm_kv(e, h=h, tt=tt, bk=bk, ws=ws):
                for f in range(NF):
                    e.matmul(bank(bk)[:, 0:256],
                             A1[:, f * T + tt * 128:f * T + (tt + 1) * 128],
                             ws[:, f * 256:f * 256 + 256],
                             start=(f == 0), stop=False)
                return e.matmul(bank(bk)[:, 0:256], ONE1F[0:1, :],
                                BKV[:, h * 256:(h + 1) * 256],
                                start=False, stop=True)
            p.step("tensor", mm_kv,
                   reads=[f"WS{h % 2}", "ONE1F", "BKV"] + A1K,
                   writes=[f"PB{bk}"], tag=f"rkv{h}.{tt}")
            p.step("vector", lambda e, h=h, tt=tt, bk=bk: e.tensor_scalar(
                A2[:, tt * 128:(tt + 1) * 128], bank(bk)[:, 0:128],
                KDG[:, h * NT + tt:h * NT + tt + 1], None, ALU.mult),
                reads=[f"PB{bk}", f"KDG{h}"],
                writes=["A2.0" if tt < 16 else "A2.x", f"rk{tt}"])
            p.step("vector", lambda e, h=h, tt=tt, bk=bk: e.tensor_copy(
                A2[:, 2048 + tt * 128:2048 + (tt + 1) * 128],
                bank(bk)[:, 128:256]),
                reads=[f"PB{bk}"], writes=["A2.1", f"rv{tt}"])
        def mm_srep(e, h=h):
            inst = None
            for tt in range(NT):
                inst = e.matmul(bank(2)[:, 0:128],
                                A2[:, tt * 128:(tt + 1) * 128],
                                A2[:, 2048 + tt * 128:2048 + (tt + 1) * 128],
                                start=(tt == 0), stop=(tt == NT - 1))
            return inst
        p.step("tensor", mm_srep,
               reads=[f"rk{tt}" for tt in range(NT)] +
                     [f"rv{tt}" for tt in range(NT)],
               writes=["PB2"], tag=f"srep{h}")
        p.step("vector", lambda e: e.tensor_copy(S32[:, :], bank(2)[:, 0:128]),
               reads=["PB2"], writes=["S32"])
        p.dma("sync", srep_loc[h, :, :], S32[:, :], reads=["S32"],
              writes=[f"pub_srep{h}"], tag=f"psr{h}")

    def dma_pub_srep(e):
        return e.dma_start(out=sh_srep[prow(e), :, :, :], in_=srep_loc[:, :, :])
    p.step("sync", dma_pub_srep,
           reads=[f"pub_srep{h}" for h in range(H)],
           writes=["pub_srep_sh"], dma=True, tag="pubsrep")

    def dma_flag1(e):
        return e.dma_start(out=sh_flags[prow(e), 0:1, 0:1], in_=NON[0:1, 0:1])
    p.step("sync", dma_flag1, reads=["NON", "pub_srep_sh"],
           writes=["pub_f1"], dma=True, tag="flag1")

    PHASE_MARKS["P1a_end"] = len(p.steps)
    ln_stats(xo_tile, ["xown"])
    ln_apply(xo_tile, ["xown"], a1t, lambda ft: f"A1.{ft}")

    for h in range(H):
        ws = WS[h % 2]
        qh = QH[h % 2]
        p.dma("gpsimd", ws[:, 0:NF * 128],
              wq_r[h].rearrange("a p c -> p a c"),
              reads=["wqkv"], writes=[f"WS{h % 2}"], tag=f"owq{h}")
        for cq in range(4):
            bk = 4 + cq % 2
            def mm_q(e, h=h, cq=cq, bk=bk, ws=ws):
                inst = None
                for f in range(NF):
                    inst = e.matmul(bank(bk)[:, :],
                                    ws[:, f * 128:f * 128 + 128],
                                    A1[:, f * T + cq * 512:f * T + (cq + 1) * 512],
                                    start=(f == 0), stop=(f == NF - 1))
                return inst
            p.step("tensor", mm_q, reads=[f"WS{h % 2}"] + A1K,
                   writes=[f"PB{bk}"], tag=f"q{h}.{cq}")
            p.step("scalar", lambda e, h=h, cq=cq, bk=bk, qh=qh: e.activation(
                qh[:, cq * 512:(cq + 1) * 512], bank(bk)[:, :], AF.Identity,
                bias=BQ[:, h:h + 1], scale=1.0),
                reads=[f"PB{bk}", "BQ"], writes=[f"QH{h % 2}"])
        p.dma("sync", q_spill[h, :, :], qh[:, :],
              reads=[f"QH{h % 2}"], writes=[f"qsp{h}"], tag=f"qsp{h}")
        p.dma("gpsimd", ws[:, 0:NF * 256],
              wkv_r[h].rearrange("a p c -> p a c"),
              reads=["wqkv"], writes=[f"WS{h % 2}"], tag=f"owkv{h}")
        for tt in range(NT):
            bk = tt % 2
            def mm_kv2(e, h=h, tt=tt, bk=bk, ws=ws):
                for f in range(NF):
                    e.matmul(bank(bk)[:, 0:256],
                             A1[:, f * T + tt * 128:f * T + (tt + 1) * 128],
                             ws[:, f * 256:f * 256 + 256],
                             start=(f == 0), stop=False)
                return e.matmul(bank(bk)[:, 0:256], ONE1F[0:1, :],
                                BKV[:, h * 256:(h + 1) * 256],
                                start=False, stop=True)
            p.step("tensor", mm_kv2,
                   reads=[f"WS{h % 2}", "ONE1F", "BKV"] + A1K,
                   writes=[f"PB{bk}"], tag=f"okv{h}.{tt}")
            ab = (h % 2) * 3
            p.step("vector", lambda e, tt=tt, bk=bk, ab=ab: e.tensor_copy(
                a2t(ab)[:, tt * 128:(tt + 1) * 128], bank(bk)[:, 0:128]),
                reads=[f"PB{bk}"], writes=[f"A2.{ab}", f"kT{h % 2}.{tt}"])
            p.step("vector", lambda e, h=h, tt=tt, bk=bk, ab=ab: e.tensor_scalar(
                a2t(ab + 1)[:, tt * 128:(tt + 1) * 128],
                bank(bk)[:, 0:128], KDV[:, h:h + 1], None, ALU.mult),
                reads=[f"PB{bk}", f"KDV{h}"],
                writes=[f"A2.{ab + 1}", f"kD{h % 2}.{tt}"])
            p.step("vector", lambda e, tt=tt, bk=bk, ab=ab: e.tensor_copy(
                a2t(ab + 2)[:, tt * 128:(tt + 1) * 128],
                bank(bk)[:, 128:256]),
                reads=[f"PB{bk}"], writes=[f"A2.{ab + 2}", f"vT{h % 2}.{tt}"])

        for tt in range(NT):
            ab = (h % 2) * 3
            bk = 6 + tt % 2
            p.step("tensor", lambda e, tt=tt, ab=ab, bk=bk: e.transpose(
                bankbf(bk)[:, 0:128], a2t(ab)[:, tt * 128:(tt + 1) * 128],
                IDN[:, :]),
                reads=[f"kT{h % 2}.{tt}", "IDN"], writes=[f"PB{bk}"],
                tag=f"ktr{h}.{tt}")
            p.step("vector", lambda e, tt=tt, bk=bk: e.tensor_copy(
                SMB[:, tt * 128:(tt + 1) * 128], bankbf(bk)[:, 0:128]),
                reads=[f"PB{bk}"], writes=["SMB"])

        p.step("gpsimd", lambda e: e.memset(S32[:, :], 0.0), writes=["S32"])
        p.step("gpsimd", lambda e: e.memset(SBF[:, :], 0.0), writes=["SBF"])
        for j in range(NCH):
            tt = j // 2
            half = (j % 2) * 64
            ab = (h % 2) * 3
            c0 = j * 64
            sb_s = 2 + j % 2
            sb_o = 4 + j % 2
            sb_d = 6 + j % 2
            p.step("tensor", lambda e, c0=c0, half=half, sb_s=sb_s, h=h: e.matmul(
                bank(sb_s)[half:half + 64, 0:64], SMB[:, c0:c0 + 64],
                QH[h % 2][:, c0:c0 + 64], start=True, stop=True),
                reads=["SMB", f"QH{h % 2}"], writes=[f"PB{sb_s}"],
                tag=f"sc{h}.{j}")
            p.step("vector", lambda e, h=h, j=j, half=half, sb_s=sb_s: e.tensor_tensor(
                SC[half:half + 64, 0:64], bank(sb_s)[half:half + 64, 0:64],
                IDT2[half:half + 64, h * 64:(h + 1) * 64], ALU.mult),
                reads=[f"PB{sb_s}", f"IDT{h}"], writes=[f"SC{j % 2}"])
            p.step("vector", lambda e, h=h, j=j, c0=c0: e.tensor_tensor(
                QD[:, (j % 2) * 64:(j % 2) * 64 + 64], QH[h % 2][:, c0:c0 + 64],
                QDB[:, h * 64:(h + 1) * 64], ALU.mult),
                reads=[f"QH{h % 2}", f"QDB{h}"], writes=[f"QD{j % 2}"])
            def mm_o(e, j=j, tt=tt, half=half, ab=ab, sb_o=sb_o):
                m1 = e.matmul(
                    bank(sb_o)[:, 0:64],
                    a2t(ab + 2)[half:half + 64, tt * 128:(tt + 1) * 128],
                    SC[half:half + 64, 0:64],
                    start=True, stop=(j == 0))
                if j == 0:
                    return m1
                return e.matmul(bank(sb_o)[:, 0:64], SBF[:, :],
                                QD[:, (j % 2) * 64:(j % 2) * 64 + 64],
                                start=False, stop=True)
            p.step("tensor", mm_o,
                   reads=[f"vT{h % 2}.{tt}", f"SC{j % 2}", f"QD{j % 2}", "SBF"],
                   writes=[f"PB{sb_o}"], tag=f"o{h}.{j}")
            p.step("tensor", lambda e, j=j, tt=tt, half=half, ab=ab, sb_d=sb_d: e.matmul(
                bank(sb_d)[:, 0:128],
                a2t(ab + 1)[half:half + 64, tt * 128:(tt + 1) * 128],
                a2t(ab + 2)[half:half + 64, tt * 128:(tt + 1) * 128],
                start=True, stop=True),
                reads=[f"kD{h % 2}.{tt}", f"vT{h % 2}.{tt}"],
                writes=[f"PB{sb_d}"], tag=f"ds{h}.{j}")
            p.step("vector", lambda e, h=h, sb_d=sb_d: e.scalar_tensor_tensor(
                S32[:, :], S32[:, :], LAM[:, h:h + 1], bank(sb_d)[:, 0:128],
                ALU.mult, ALU.add),
                reads=["S32", f"LAM{h}", f"PB{sb_d}"], writes=["S32"])
            p.step("scalar", lambda e: e.copy(SBF[:, :], S32[:, :]),
                   reads=["S32"], writes=["SBF"])
            p.step("scalar", lambda e, c0=c0, sb_o=sb_o: e.copy(
                AH[:, c0:c0 + 64], bank(sb_o)[:, 0:64]),
                reads=[f"PB{sb_o}"], writes=["AH"])
        p.dma("sync", attn_spill[h, :, :], AH[:, :],
              reads=["AH"], writes=[f"asp{h}"], tag=f"asp{h}")
        p.dma("sync", sloc_loc[h, :, :], S32[:, :], reads=["S32"],
              writes=[f"pub_sloc{h}"], tag=f"psl{h}")

    def dma_pub_sloc(e):
        return e.dma_start(out=sh_sloc[prow(e), :, :, :], in_=sloc_loc[:, :, :])
    p.step("sync", dma_pub_sloc,
           reads=[f"pub_sloc{h}" for h in range(H)],
           writes=["pub_sloc_sh"], dma=True, tag="pubsloc")

    def dma_flag2(e):
        return e.dma_start(out=sh_flags[prow(e), 0:1, 1:2], in_=NON[0:1, 1:2])
    p.step("sync", dma_flag2, reads=["NON", "pub_sloc_sh"],
           writes=["pub_f2"], dma=True, tag="flag2")

    PHASE_MARKS["P1b_end"] = len(p.steps)
    psem = nc.alloc_semaphore("E_poll")

    def poll_partner(e):
        rpart = prow_p(e)
        cnt = reg(e, "pollcnt")
        more = reg(e, "pollmore")
        eq = reg(e, "polleq")
        fv = reg(e, "pollfv")
        nn = reg(e, "pollnn")
        e.reg_mov(cnt, 0)

        def cond():
            e.reg_add(cnt, cnt, 16)
            e.dma_start(out=FLG[:, :], in_=sh_flags[rpart, 0:1, :]).then_inc(psem, 16)
            e.wait_ge(psem, cnt)
            e.reg_mov(more, 0)
            for i in range(2):
                e.reg_load(fv, FLG[0:1, i:i + 1])
                e.reg_load(nn, NON[0:1, i:i + 1])
                e.reg_alu(eq, fv, nn, ALU.is_equal)
                e.reg_alu(more, more, eq, ALU.add)
            e.reg_alu(more, more, 2, ALU.is_lt)
            e.reg_alu(eq, cnt, 16 * 3000000, ALU.is_lt)
            e.reg_mul(more, more, eq)
            return more

        with e.While(cond):
            pass
        return e.nop()
    p.step("sync", poll_partner, reads=["NON", "pub_f1", "pub_f2"],
           writes=["partner", "FLG"], tag="poll")

    def dma_fetch_psrep(e):
        return e.dma_start(out=psrep_loc[:, :, :], in_=sh_srep[prow_p(e), :, :, :])
    p.step("sync", dma_fetch_psrep, reads=["partner"], writes=["psrep"],
           dma=True, tag="fpsrep")

    def dma_fetch_psloc(e):
        return e.dma_start(out=psloc_loc[:, :, :], in_=sh_sloc[prow_p(e), :, :, :])
    p.step("sync", dma_fetch_psloc, reads=["partner"], writes=["psloc"],
           dma=True, tag="fpsloc")

    for h in range(H):
        qh = QH[h % 2]
        p.dma("sync", SPT[0][:, :], srep_loc[h, :, :],
              reads=[f"pub_srep{h}"], writes=["SPT0"])
        p.dma("sync", SPT[2][:, :], psrep_loc[h, :, :],
              reads=["psrep"], writes=["SPT2"])
        p.dma("sync", SPT[3][:, :], psloc_loc[h, :, :],
              reads=["psloc"], writes=["SPT3"])
        p.step("vector", lambda e, h=h: e.tensor_scalar(
            SST[:, :], SPT[3][:, :], EXF[:, 3 * h:3 * h + 1], None, ALU.mult),
            reads=["SPT3", f"EXF{h}"], writes=["SST"])
        p.step("vector", lambda e, h=h: e.scalar_tensor_tensor(
            SST[:, :], SPT[0][:, :], EXF[:, 3 * h + 1:3 * h + 2], SST[:, :],
            ALU.mult, ALU.add), reads=["SPT0", f"EXF{h}", "SST"], writes=["SST"])
        p.step("vector", lambda e, h=h: e.scalar_tensor_tensor(
            SST[:, :], SPT[2][:, :], EXF[:, 3 * h + 2:3 * h + 3], SST[:, :],
            ALU.mult, ALU.add), reads=["SPT2", f"EXF{h}", "SST"], writes=["SST"])
        p.step("scalar", lambda e: e.copy(SSB[:, :], SST[:, :]),
               reads=["SST"], writes=["SSB"])
        p.step("vector", lambda e, h=h: e.tensor_scalar(
            TMP[64:65, 900:901], SLP[64:65, h:h + 1], -1.0, None, ALU.mult),
            reads=["SLP"], writes=["TMPn"])
        p.step("vector", lambda e: e.tensor_scalar(
            QGR[:, :], TMQ[:, :], TMP[64:65, 900:901], None, ALU.mult),
            reads=["TMQpos", "TMPn"], writes=["QGR"])
        p.step("scalar", lambda e: e.activation(
            QGR[:, :], QGR[:, :], AF.Exp),
            reads=["QGR"], writes=["QGR"])
        def mm_qg(e):
            inst = None
            for c in range(4):
                inst = e.matmul(bank(c)[:, :], ONESB[64:65, :],
                                QGR[:, c * 512:(c + 1) * 512],
                                start=True, stop=True)
            return inst
        p.step("tensor", mm_qg, reads=["QGR", "ONESB"],
               writes=["PB0", "PB1", "PB2", "PB3"])
        p.dma("sync", qh[:, :], q_spill[h, :, :],
              reads=[f"qsp{h}"], writes=[f"QH{h % 2}"])
        for c in range(4):
            p.step("vector", lambda e, c=c, qh=qh, h=h: e.tensor_tensor(
                SMB[:, c * 512:(c + 1) * 512], qh[:, c * 512:(c + 1) * 512],
                bank(c)[:, :], ALU.mult),
                reads=[f"QH{h % 2}", f"PB{c}"], writes=["SMB"])
        p.dma("sync", AH[:, :], attn_spill[h, :, :],
              reads=[f"asp{h}"], writes=["AH"])
        for c in range(4):
            bk = 4 + c % 2
            p.step("tensor", lambda e, c=c, bk=bk: e.matmul(
                bank(bk)[:, :], SSB[:, :], SMB[:, c * 512:(c + 1) * 512],
                start=True, stop=True),
                reads=["SSB", "SMB"], writes=[f"PB{bk}"], tag=f"corr{h}.{c}")
            p.step("vector", lambda e, h=h, c=c, bk=bk: e.tensor_tensor(
                A1[:, h * T + c * 512:h * T + (c + 1) * 512],
                AH[:, c * 512:(c + 1) * 512], bank(bk)[:, :], ALU.add),
                reads=["AH", f"PB{bk}"], writes=[f"A1.{h}"], tag=f"cv{h}.{c}")

    PHASE_MARKS["P1c_end"] = len(p.steps)
    for tq in range(8):
        c0 = tq * 256
        for hf in range(2):
            for f in range(NF):
                ws = WS[f % 2]
                p.dma("gpsimd", ws[:, 0:2048], wout_r[f, :, :],
                      reads=["wout"], writes=[f"WS{f % 2}"],
                      tag=f"wo{tq}.{hf}.{f}")
                def mm_op(e, f=f, c0=c0, ws=ws, hf=hf):
                    inst = None
                    for b_ in range(8):
                        fo = 2 * b_ + hf
                        inst = e.matmul(
                            bank(b_)[:, hf * 256:hf * 256 + 256],
                            ws[:, fo * 128:(fo + 1) * 128],
                            A1[:, f * T + c0:f * T + c0 + 256],
                            start=(f == 0), stop=(f == NF - 1))
                    return inst
                wr = ALL_PB + ["PBs"] if f in (0, NF - 1) else ["PBs"]
                p.step("tensor", mm_op,
                       reads=[f"WS{f % 2}", f"A1.{f}"], writes=wr,
                       tag=f"op{tq}.{hf}.{f}")
            for b_ in range(8):
                fo = 2 * b_ + hf
                p.step("scalar", lambda e, fo=fo, b_=b_, hf=hf: e.activation(
                    OS[fo % 2][:, 0:256],
                    bank(b_)[:, hf * 256:hf * 256 + 256],
                    AF.Identity, bias=BO[:, fo:fo + 1], scale=1.0),
                    reads=[f"PB{b_}", "BO"], writes=[f"OS{fo % 2}"])
                p.dma("sync", ao_scr[fo, :, c0:c0 + 256], OS[fo % 2][:, 0:256],
                      reads=[f"OS{fo % 2}"], writes=[f"ao{fo}"],
                      tag=f"aod{tq}.{fo}")

    PHASE_MARKS["P1d_end"] = len(p.steps)
    AOK = [f"ao{fo}" for fo in range(NF)]

    def ao_tile(ft):
        return ao_scr[ft, :, :]
    ln_stats(ao_tile, AOK)
    for ft in range(NF):
        p.dma("sync", XS[ft % 2][:, :], ao_tile(ft),
              reads=AOK, writes=[f"XS{ft % 2}"])
        p.dma("sync", WS[ft % 2].bitcast(F32)[:, 0:2048],
              xo_tile(ft), reads=["xown"], writes=[f"WS{ft % 2}"])
        for c in range(4):
            p.step("vector", lambda e, ft=ft, c=c: e.tensor_tensor(
                OS[ft % 2][:, c * 512:(c + 1) * 512],
                XS[ft % 2][:, c * 512:(c + 1) * 512], bank(c)[:, :],
                ALU.subtract),
                reads=[f"XS{ft % 2}", f"PB{c}"], writes=[f"OS{ft % 2}"])
            p.step("vector", lambda e, ft=ft, c=c: e.tensor_tensor(
                OS[ft % 2][:, c * 512:(c + 1) * 512],
                OS[ft % 2][:, c * 512:(c + 1) * 512], bank(4 + c)[:, :],
                ALU.mult),
                reads=[f"OS{ft % 2}", f"PB{4 + c}"], writes=[f"OS{ft % 2}"])
            p.step("vector", lambda e, ft=ft, c=c: e.tensor_tensor(
                OS[ft % 2][:, c * 512:(c + 1) * 512],
                OS[ft % 2][:, c * 512:(c + 1) * 512],
                WS[ft % 2].bitcast(F32)[:, c * 512:(c + 1) * 512], ALU.add),
                reads=[f"OS{ft % 2}", f"WS{ft % 2}"], writes=[f"OS{ft % 2}"])
        p.dma("sync", x1_scr[ft, :, :], OS[ft % 2][:, :],
              reads=[f"OS{ft % 2}"], writes=[f"x1.{ft}"], tag=f"x1d{ft}")

    X1K = [f"x1.{ft}" for ft in range(NF)]

    PHASE_MARKS["P1f_end"] = len(p.steps)
    def x1_tile(ft):
        return x1_scr[ft, :, :]
    ln_stats(x1_tile, X1K)
    ln_apply(x1_tile, X1K, a1t, lambda ft: f"A1.{ft}")

    PHASE_MARKS["P2a_end"] = len(p.steps)
    for tt in range(NT):
        def mm_lg(e, tt=tt):
            inst = None
            for f in range(NF):
                inst = e.matmul(bank(6)[:, 0:16],
                                A1[:, f * T + tt * 128:f * T + (tt + 1) * 128],
                                GW[:, f * E:(f + 1) * E],
                                start=(f == 0), stop=(f == NF - 1))
            return inst
        p.step("tensor", mm_lg, reads=["GW"] + A1K, writes=["PB6"],
               tag=f"lg{tt}")
        p.step("vector", lambda e: e.reduce_max(
            TMP[:, 960:961], bank(6)[:, 0:16], mybir.AxisListType.X,
            negate=True), reads=["PB6"], writes=["TMPmx"])
        p.step("scalar", lambda e: e.activation(
            PRB[:, :], bank(6)[:, 0:16], AF.Exp, bias=TMP[:, 960:961],
            scale=1.0), reads=["PB6", "TMPmx"], writes=["PRB"])
        p.step("vector", lambda e: e.reduce_sum(
            TMP[:, 961:962], PRB[:, :], mybir.AxisListType.X),
            reads=["PRB"], writes=["TMPsm"])
        p.step("vector", lambda e: e.reciprocal(TMP[:, 962:963], TMP[:, 961:962]),
               reads=["TMPsm"], writes=["TMPrc"])
        p.step("vector", lambda e: e.tensor_scalar(
            PRB[:, :], PRB[:, :], TMP[:, 962:963], None, ALU.mult),
            reads=["PRB", "TMPrc"], writes=["PRB"])
        p.step("vector", lambda e: e.max(out=MX8[:, :], in_=PRB[:, :]),
               reads=["PRB"], writes=["MX8"])
        p.step("vector", lambda e: e.memset(MX8[:, 6:8], 0.0),
               reads=["MX8"], writes=["MX8"])
        p.step("vector", lambda e: e.match_replace(
            out=CMB[:, :], in_to_replace=MX8[:, :], in_values=PRB[:, :],
            imm_value=0.0), reads=["MX8", "PRB"], writes=["CMB", "MX8"])
        p.step("vector", lambda e: e.tensor_sub(CMB[:, :], PRB[:, :], CMB[:, :]),
               reads=["PRB", "CMB"], writes=["CMB"])
        p.step("tensor", lambda e: e.transpose(
            bank(7)[0:16, 0:128], CMB[:, :], IDNF[:, :]),
            reads=["CMB", "IDNF"], writes=["PB7"], tag=f"cmt{tt}")
        p.step("vector", lambda e: e.tensor_copy(
            TMP[0:16, 640:768], bank(7)[0:16, 0:128]),
            reads=["PB7"], writes=["CMTt"])
        p.step("tensor", lambda e: e.matmul(
            bank(6)[32:48, 0:128], IDNF[0:16, 0:16], TMP[0:16, 640:768],
            start=True, stop=True),
            reads=["CMTt", "IDNF"], writes=["PB6"], tag=f"cmts{tt}")
        p.step("vector", lambda e, tt=tt: e.tensor_copy(
            CMT[:, tt * 128:(tt + 1) * 128], bank(6)[32:48, 0:128]),
            reads=["PB6"], writes=["CMT"])

    PHASE_MARKS["P2b_end"] = len(p.steps)
    def b1_block(widx_dma, n_pairs, dst_scr, use_comb, eidx):
        for i in range(n_pairs):
            ws = WS[i % 2]
            widx_dma(i, ws)
            for cq in range(4):
                bkg = cq % 2
                bku = 2 + cq % 2
                def mm_g(e, i=i, cq=cq, ws=ws, bkg=bkg):
                    inst = None
                    for f in range(NF):
                        inst = e.matmul(bank(bkg)[:, :],
                                        ws[:, f * 256:f * 256 + 128],
                                        A1[:, f * T + cq * 512:f * T + (cq + 1) * 512],
                                        start=(f == 0), stop=(f == NF - 1))
                    return inst
                p.step("tensor", mm_g, reads=[f"WS{i % 2}"] + A1K,
                       writes=[f"PB{bkg}"], tag=f"g{eidx}.{i}.{cq}")
                def mm_u(e, i=i, cq=cq, ws=ws, bku=bku):
                    inst = None
                    for f in range(NF):
                        inst = e.matmul(bank(bku)[:, :],
                                        ws[:, f * 256 + 128:f * 256 + 256],
                                        A1[:, f * T + cq * 512:f * T + (cq + 1) * 512],
                                        start=(f == 0), stop=(f == NF - 1))
                    return inst
                p.step("tensor", mm_u, reads=[f"WS{i % 2}"] + A1K,
                       writes=[f"PB{bku}"], tag=f"u{eidx}.{i}.{cq}")
                p.step("scalar", lambda e, cq=cq, bkg=bkg: e.activation(
                    OS[0][:, cq * 512:(cq + 1) * 512], bank(bkg)[:, :], AF.Silu),
                    reads=[f"PB{bkg}"], writes=["OS0"])
                if use_comb:
                    p.step("vector", lambda e, cq=cq, bku=bku: e.tensor_tensor(
                        OS[1][:, cq * 512:(cq + 1) * 512], bank(bku)[:, :],
                        XS[0][:, cq * 512:(cq + 1) * 512], ALU.mult),
                        reads=[f"PB{bku}", "XS0"], writes=["OS1"])
                else:
                    p.step("vector", lambda e, cq=cq, bku=bku: e.tensor_copy(
                        OS[1][:, cq * 512:(cq + 1) * 512], bank(bku)[:, :]),
                        reads=[f"PB{bku}"], writes=["OS1"])
                p.step("vector", lambda e, cq=cq, i=i: e.tensor_tensor(
                    QH[i % 2][:, cq * 512:(cq + 1) * 512],
                    OS[0][:, cq * 512:(cq + 1) * 512],
                    OS[1][:, cq * 512:(cq + 1) * 512], ALU.mult),
                    reads=["OS0", "OS1"], writes=[f"QH{i % 2}"])
            p.dma("sync", dst_scr(i), QH[i % 2][:, :],
                  reads=[f"QH{i % 2}"], writes=[f"b1d{eidx}"],
                  tag=f"b1d{eidx}.{i}")

    def sw1_dma(i, ws):
        p.dma("gpsimd", ws[:, 0:NF * 256],
              sw1_r[i],
              reads=["sw1"], writes=[f"WS{i % 2}"], tag=f"sw1.{i}")
    b1_block(sw1_dma, NIS, lambda i: hsh_scr[i, :, :], False, "sh")

    for ex in range(E):
        def mm_bc2(e, ex=ex):
            inst = None
            for c in range(4):
                inst = e.matmul(bank(4 + c)[:, :],
                                SEL[:, ex * 128:(ex + 1) * 128],
                                CMT[:, c * 512:(c + 1) * 512],
                                start=True, stop=True)
            return inst
        p.step("tensor", mm_bc2, reads=["SEL", "CMT"],
               writes=["PB4", "PB5", "PB6", "PB7"], tag=f"bc{ex}")
        for c in range(4):
            p.step("vector", lambda e, c=c: e.tensor_copy(
                XS[0][:, c * 512:(c + 1) * 512], bank(4 + c)[:, :]),
                reads=[f"PB{4 + c}"], writes=["XS0"])
        def w1_dma(i, ws, ex=ex):
            p.dma("gpsimd", ws[:, 0:NF * 256],
                  w1_r[ex, i],
                  reads=["w1"], writes=[f"WS{i % 2}"], tag=f"w1.{ex}.{i}")
        b1_block(w1_dma, NI, lambda i, ex=ex: actw_scr[ex, i, :, :], True, f"e{ex}")

    PHASE_MARKS["P2c_end"] = len(p.steps)
    def slot_tile(sl, j):
        if sl == 0:
            return a1t(j), f"A1.{j}"
        if j < 5:
            return a1t(11 + j), f"A1.{11 + j}"
        return a2t(j - 5), f"A2.{j - 5}"

    def b2_pass(pi, n_i, src_tiles_dma, w_dma, accum, eidx):
        sl = pi % 2
        keys = []
        for j in range(n_i):
            dst, key = slot_tile(sl, j) if n_i == NI else (None, None)
            if n_i == NIS:
                if j < 11:
                    dst, key = slot_tile(0, j)
                else:
                    dst, key = slot_tile(1, j - 11)
            src_tiles_dma(j, dst, key)
            keys.append(key)
        for fo in range(NF):
            ws = WS[fo % 2]
            w_dma(fo, ws)
            for cq in range(4):
                bk = cq % 2 + (2 if fo % 2 else 0)
                def mm_y(e, fo=fo, cq=cq, ws=ws, bk=bk, n_i=n_i, sl=sl):
                    inst = None
                    for j in range(n_i):
                        if n_i == NIS:
                            srct, _ = slot_tile(0, j) if j < 11 else slot_tile(1, j - 11)
                        else:
                            srct, _ = slot_tile(sl, j)
                        inst = e.matmul(bank(bk)[:, :],
                                        ws[:, j * 128:(j + 1) * 128],
                                        srct[:, cq * 512:(cq + 1) * 512],
                                        start=(j == 0), stop=(j == n_i - 1))
                    return inst
                p.step("tensor", mm_y, reads=[f"WS{fo % 2}"] + keys,
                       writes=[f"PB{bk}"], tag=f"y{eidx}.{fo}.{cq}")
                p.step("vector", lambda e, cq=cq, bk=bk, fo=fo: e.tensor_copy(
                    OS[fo % 2][:, cq * 512:(cq + 1) * 512], bank(bk)[:, :]),
                    reads=[f"PB{bk}"], writes=[f"OS{fo % 2}"])
            p.dma("gpsimd", y_scr[fo, :, :], OS[fo % 2][:, :],
                  reads=[f"OS{fo % 2}"], writes=[f"y.{fo}"],
                  tag=f"yd{eidx}.{fo}",
                  **({"accum_op": ALU.add} if accum else {}))

    def hsh_dma(j, dst, key):
        p.dma("sync", dst, hsh_scr[j, :, :],
              reads=["b1dsh"], writes=[key], tag=f"hshl{j}")

    def sw2_dma(fo, ws):
        p.dma("gpsimd", ws[:, 0:NIS * 128],
              sw2_r[fo],
              reads=["sw2"], writes=[f"WS{fo % 2}"], tag=f"sw2.{fo}")
    b2_pass(0, NIS, hsh_dma, sw2_dma, False, "ysh")

    for ex in range(E):
        def actw_dma(j, dst, key, ex=ex):
            p.dma("sync", dst, actw_scr[ex, j, :, :],
                  reads=[f"b1de{ex}"], writes=[key], tag=f"awl{ex}.{j}")
        def w2_dma(fo, ws, ex=ex):
            p.dma("gpsimd", ws[:, 0:NI * 128],
                  w2_r[ex, fo],
                  reads=["w2"], writes=[f"WS{fo % 2}"], tag=f"w2.{ex}.{fo}")
        b2_pass(1 + ex, NI, actw_dma, w2_dma, True, f"ye{ex}")

    YK = [f"y.{fo}" for fo in range(NF)]

    PHASE_MARKS["P2d_end"] = len(p.steps)
    def y_tile(ft):
        return y_scr[ft, :, :]
    ln_stats(y_tile, YK)
    for ft in range(NF):
        p.dma("sync", XS[ft % 2][:, :], y_tile(ft),
              reads=YK, writes=[f"XS{ft % 2}"])
        p.dma("sync", WS[ft % 2].bitcast(F32)[:, 0:2048], x1_tile(ft),
              reads=X1K, writes=[f"WS{ft % 2}"])
        for c in range(4):
            p.step("vector", lambda e, ft=ft, c=c: e.tensor_tensor(
                OS[ft % 2][:, c * 512:(c + 1) * 512],
                XS[ft % 2][:, c * 512:(c + 1) * 512], bank(c)[:, :],
                ALU.subtract),
                reads=[f"XS{ft % 2}", f"PB{c}"], writes=[f"OS{ft % 2}"])
            p.step("vector", lambda e, ft=ft, c=c: e.tensor_tensor(
                OS[ft % 2][:, c * 512:(c + 1) * 512],
                OS[ft % 2][:, c * 512:(c + 1) * 512], bank(4 + c)[:, :],
                ALU.mult),
                reads=[f"OS{ft % 2}", f"PB{4 + c}"], writes=[f"OS{ft % 2}"])
            p.step("vector", lambda e, ft=ft, c=c: e.tensor_tensor(
                OS[ft % 2][:, c * 512:(c + 1) * 512],
                OS[ft % 2][:, c * 512:(c + 1) * 512],
                WS[ft % 2].bitcast(F32)[:, c * 512:(c + 1) * 512], ALU.add),
                reads=[f"OS{ft % 2}", f"WS{ft % 2}"], writes=[f"OS{ft % 2}"])
        p.dma("sync", yT_out[ft * 128:(ft + 1) * 128, :], OS[ft % 2][:, :],
              reads=[f"OS{ft % 2}"], writes=[f"out.{ft}"], tag=f"outd{ft}")

    p.step("sync", lambda e: e.nop(),
           reads=[f"out.{ft}" for ft in range(NF)], writes=["FINAL"],
           tag="final")

    PHASE_MARKS["end"] = len(p.steps)
    if BUILD_LIMIT is not None:
        p.steps = p.steps[:BUILD_LIMIT]
    if DEBUG_TAP is not None:
        tk = []
        if DEBUG_TAP in ("x1", "ao", "y"):
            srct = {"x1": x1_scr, "ao": ao_scr, "y": y_scr}[DEBUG_TAP]
            for ft in range(NF):
                p.dma("sync", yT_out[ft * 128:(ft + 1) * 128, :], srct[ft, :, :],
                      reads=[], writes=[f"tap{ft}"], tag=f"tap{ft}")
                tk.append(f"tap{ft}")
        elif DEBUG_TAP in ("qall", "attnall"):
            srct = {"qall": q_spill, "attnall": attn_spill}[DEBUG_TAP]
            for h in range(H):
                p.dma("gpsimd", yT_out[h * 128:(h + 1) * 128, :], srct[h, :, :],
                      reads=[], writes=[f"tap{h}"], tag=f"tap{h}")
                tk.append(f"tap{h}")
        elif DEBUG_TAP in ("sloc", "srep", "psloc", "psrep"):
            srct = {"sloc": sloc_loc, "srep": srep_loc,
                    "psloc": psloc_loc, "psrep": psrep_loc}[DEBUG_TAP]
            for h in range(H):
                p.dma("sync", yT_out[h * 128:(h + 1) * 128, 0:128], srct[h, :, :],
                      reads=[], writes=[f"tap{h}"], tag=f"tap{h}")
                tk.append(f"tap{h}")
        elif DEBUG_TAP == "idt":
            p.dma("gpsimd", yT_out[0:128, 0:H * 64], IDT[:, :],
                  reads=[f"IDT{hh}" for hh in range(H)], writes=["tapI"],
                  tag="tapI")
            p.dma("gpsimd", yT_out[0:128, 1024:1024 + H * 64], QDB[:, :],
                  reads=[f"QDB{hh}" for hh in range(H)], writes=["tapQ"],
                  tag="tapQ")
            p.dma("sync", yT_out[128:256, 0:H], KDV[:, :],
                  reads=[f"KDV{hh}" for hh in range(H)], writes=["tapK"],
                  tag="tapK")
            p.dma("sync", yT_out[128:256, 16:16 + H], LAM[:, :],
                  reads=[f"LAM{hh}" for hh in range(H)], writes=["tapL"],
                  tag="tapL")
            tk = ["tapI", "tapQ", "tapK", "tapL"]
        elif DEBUG_TAP == "xn":
            for ft in range(NF):
                p.dma("gpsimd", yT_out[ft * 128:(ft + 1) * 128, :], a1t(ft),
                      reads=[f"A1.{ft}"], writes=[f"tap{ft}"], tag=f"tap{ft}")
                tk.append(f"tap{ft}")
        p.step("sync", lambda e: e.nop(), reads=tk, writes=["FINAL2"],
               tag="final2")
    with nc.Block() as block:
        p.emit(block)
    return nc


_NC_CACHE = {}


def _get_program():
    if "nc" not in _NC_CACHE:
        _NC_CACHE["nc"] = build_program()
    return _NC_CACHE["nc"]


def _relayout_weights(inp):
    w = {}
    Wqkv = inp["L0_attn_Wqkv"]
    t = Wqkv.reshape(NF, 128, 3, H, 128)
    w["wq_r"] = np.ascontiguousarray(
        t[:, :, 0].transpose(2, 0, 1, 3))
    w["wkv_r"] = np.ascontiguousarray(
        t[:, :, 1:3].transpose(3, 0, 1, 2, 4).reshape(H, NF, 128, 256))
    bqkv = inp["L0_attn_bqkv"].reshape(3, H, 128)
    w["bq_pp"] = np.ascontiguousarray(bqkv[0].T)
    w["bkv_row"] = np.ascontiguousarray(
        bqkv[1:3].transpose(1, 0, 2).reshape(1, H * 256))
    w["wout_r"] = np.ascontiguousarray(
        inp["L0_attn_Wout"].reshape(NF, 128, D))
    w["bout_pp"] = np.ascontiguousarray(
        inp["L0_attn_bout"].reshape(NF, 128).T)
    w["slopes_pp"] = np.ascontiguousarray(
        np.broadcast_to(inp["L0_attn_slopes"][None, :], (128, H)))
    w["gate_wTd"] = np.ascontiguousarray(inp["L0_ffn_gate_w"].T)
    sw1 = inp["L0_ffn_shared_w1"]
    t = sw1.reshape(2, NIS, 128, NF, 128)
    w["sw1_r"] = np.ascontiguousarray(
        t.transpose(1, 4, 3, 0, 2).reshape(NIS, 128, NF, 256))
    sw2 = inp["L0_ffn_shared_w2"]
    t = sw2.reshape(NF, 128, NIS, 128)
    w["sw2_r"] = np.ascontiguousarray(t.transpose(0, 3, 2, 1))
    w1 = inp["L0_ffn_w1"]
    t = w1.reshape(E, 2, NI, 128, NF, 128)
    w["w1_r"] = np.ascontiguousarray(
        t.transpose(0, 2, 5, 4, 1, 3).reshape(E, NI, 128, NF, 256))
    w2 = inp["L0_ffn_w2"]
    t = w2.reshape(E, NF, 128, NI, 128)
    w["w2_r"] = np.ascontiguousarray(t.transpose(0, 1, 4, 3, 2))
    return w


def kernel(**inputs):
    x = np.asarray(inputs["L0_attn_x"], dtype=np.float32)
    wmaps = _relayout_weights(inputs)
    nonce = np.random.randint(1, 2**31 - 2, size=2, dtype=np.int32)
    nonce_arr = np.array([[nonce[0], nonce[1]]], dtype=np.int32)

    exm_tab = {0: (0.0, 0.0, 0.0), 1: (1.0, 0.0, 0.0),
               2: (0.0, 1.0, 1.0), 3: (1.0, 1.0, 1.0)}
    exd_tab = {0: (0.0, 0.0, 0.0), 1: (0.0, 0.0, 0.0),
               2: (0.0, float(T), 0.0), 3: (0.0, float(T), 2.0 * T)}

    in_maps = []
    for c in range(8):
        b = c // 4
        pp = c % 4
        o0 = pp * T
        r0 = (pp % 2) * T
        m = dict(wmaps)
        m["xT_own"] = np.ascontiguousarray(x[b, o0:o0 + T, :].T)
        m["xT_rep"] = np.ascontiguousarray(x[b, r0:r0 + T, :].T)
        m["exch_m"] = np.ascontiguousarray(
            np.broadcast_to(np.array(exm_tab[pp], np.float32)[None, :], (128, 3)))
        m["exch_d"] = np.ascontiguousarray(
            np.broadcast_to(np.array(exd_tab[pp], np.float32)[None, :], (128, 3)))
        m["nonce_in"] = nonce_arr
        in_maps.append(m)

    nc = _get_program()
    res = run_bass_kernel_spmd(nc, in_maps, core_ids=list(range(8)))

    out = np.empty((B, N_SEQ, D), dtype=np.float32)
    for c in range(8):
        b = c // 4
        o0 = (c % 4) * T
        out[b, o0:o0 + T, :] = res.results[c]["yT_out"].T
    return out
